# version 13
# baseline (speedup 1.0000x reference)
import numpy as np
import ml_dtypes
from contextlib import ExitStack

import concourse.bass as bass
import concourse.mybir as mybir
from concourse.bass_utils import run_bass_kernel_spmd

F32 = mybir.dt.float32
BF16 = mybir.dt.bfloat16
AF = mybir.ActivationFunctionType
ALU = mybir.AluOpType

D = 1024
EC = 8
GK = 1024
KC = GK // 128

ENGINES = ("tensor", "vector", "scalar", "sync", "gpsimd")


class Prog:

    def __init__(self, nc, es):
        self.nc = nc
        self.es = es
        self.ops = []
        self.sems = {}
        self.cnt = {}

    def ch(self, name):
        if name not in self.sems:
            self.sems[name] = self.es.enter_context(self.nc.semaphore(name))
            self.cnt[name] = 0
        return name

    def op(self, eng, build, waits=(), inc=None, amt=1, drain=True):
        waits = tuple((c, v) for w in waits if w is not None for (c, v) in [w])
        for c, _ in waits:
            self.ch(c)
        ev = None
        if inc is not None:
            self.ch(inc)
            self.cnt[inc] += amt
            ev = (inc, self.cnt[inc])
        self.ops.append(dict(eng=eng, build=build, waits=waits,
                             inc=(inc, amt) if inc else None, is_dma=(amt == 16),
                             drain=(drain and eng == "vector")))
        return ev

    def dma(self, eng, out, in_, waits=(), ch=None, transpose=False):
        if transpose:
            def build(e, out=out, in_=in_):
                return e.dma_start_transpose(out=out, in_=in_)
        else:
            def build(e, out=out, in_=in_):
                return e.dma_start(out=out, in_=in_)
        return self.op(eng, build, waits=waits, inc=ch, amt=16)

    def _batch_incs(self):
        waited = {}
        for o in self.ops:
            for c, v in o["waits"]:
                waited.setdefault(c, set()).add(v)
        inc_engs = {}
        for o in self.ops:
            if o["inc"]:
                inc_engs.setdefault(o["inc"][0], set()).add(o["eng"])
        cum = {}
        pend = {}
        last_keep = {}
        for o in self.ops:
            if not o["inc"] or o["is_dma"] or o["eng"] == "tensor" \
                    or o["eng"] == "sync":
                continue
            c, a = o["inc"]
            if len(inc_engs[c]) > 1:
                continue
            lo = cum.get(c, 0)
            hi = lo + a
            cum[c] = hi
            if any(lo < t <= hi for t in waited.get(c, ())):
                o["inc"] = (c, a + pend.pop(c, 0))
                last_keep[c] = o
            else:
                pend[c] = pend.get(c, 0) + a
                o["inc"] = None
        for c, a in pend.items():
            if c in last_keep:
                ch, aa = last_keep[c]["inc"]
                last_keep[c]["inc"] = (ch, aa + a)

    def check_deadlock(self):
        q = {e: [o for o in self.ops if o["eng"] == e] for e in ENGINES}
        pos = {e: 0 for e in ENGINES}
        val = {c: 0 for c in self.cnt}
        progress = True
        while progress:
            progress = False
            for e in ENGINES:
                while pos[e] < len(q[e]):
                    o = q[e][pos[e]]
                    if all(val[c] >= v for c, v in o["waits"]):
                        if o["inc"]:
                            c, a = o["inc"]
                            val[c] += a
                        pos[e] += 1
                        progress = True
                    else:
                        break
        if any(pos[e] < len(q[e]) for e in ENGINES):
            msg = []
            for e in ENGINES:
                if pos[e] < len(q[e]):
                    o = q[e][pos[e]]
                    bad = [(c, v, val[c]) for c, v in o["waits"] if val[c] < v]
                    msg.append(f"{e}@{pos[e]}/{len(q[e])} needs {bad}")
            raise RuntimeError("deadlock: " + "; ".join(msg))

    def emit(self, block):
        self._batch_incs()
        self.check_deadlock()
        sems = self.sems
        for e in ENGINES:
            myops = [o for o in self.ops if o["eng"] == e]

            def body(eng, e=e, myops=myops):
                clean = False
                for o in myops:
                    for c, v in o["waits"]:
                        eng.wait_ge(sems[c], v)
                    if o.get("drain") and not clean:
                        eng.drain()
                    ins = o["build"](eng)
                    clean = False
                    if o["inc"]:
                        c, a = o["inc"]
                        if e in ("vector", "scalar", "gpsimd") and not o["is_dma"]:
                            eng.drain().then_inc(sems[c], a)
                            clean = True
                        else:
                            ins.then_inc(sems[c], a)
            getattr(block, e)(body)


def build_nc(seq, tok, dbg=False):
    nb = tok // 128
    bh = nb // 2
    ng = seq // GK
    hw = bh * 128
    nst = tok // 512
    nqt = tok // 512
    ntile = tok // 128
    nhalfg = seq // 512

    nc = bass.Bass("TRN2", target_bir_lowering=False)

    xq = nc.dram_tensor("xq", [tok, D], F32, kind="ExternalInput").ap()
    xk = nc.dram_tensor("xk", [seq, D], F32, kind="ExternalInput").ap()
    rw = nc.dram_tensor("rw", [D, D], F32, kind="ExternalInput").ap()
    w1 = nc.dram_tensor("w1", [D, D], F32, kind="ExternalInput").ap()
    w2 = nc.dram_tensor("w2", [D, D], F32, kind="ExternalInput").ap()
    w3 = nc.dram_tensor("w3", [D, D], F32, kind="ExternalInput").ap()
    bi = nc.dram_tensor("bi", [1, D], F32, kind="ExternalInput").ap()
    mk = nc.dram_tensor("mk", [2 * KC * 128, 128], BF16, kind="ExternalInput").ap()
    out = nc.dram_tensor("out", [tok, D], F32, kind="ExternalOutput").ap()

    sk = "ExternalOutput" if dbg else "Internal"
    ek_tm = nc.dram_tensor("ek_tm", [seq, D], BF16, kind=sk).ap()
    ek_fm = nc.dram_tensor("ek_fm", [EC * 128, seq], BF16, kind="Internal").ap()
    eq_fm = nc.dram_tensor("eq_fm", [EC * 128, tok], BF16, kind="Internal").ap()
    eq_tm = nc.dram_tensor("eq_tm", [tok, D], BF16, kind=sk).ap()
    u_tm = nc.dram_tensor("u_tm", [tok, D], BF16, kind=sk).ap()
    xp_d = nc.dram_tensor("xp_d", [tok, D], F32, kind=sk).ap()
    if dbg:
        dq = nc.dram_tensor("dq", [128, EC * tok], BF16, kind="ExternalOutput").ap()
        dpt = nc.dram_tensor("dpt", [2, 128, KC * (tok // 2)], BF16, kind="ExternalOutput").ap()
        doa = nc.dram_tensor("doa", [2, 128, (tok // 2) * 8], F32, kind="ExternalOutput").ap()
        dde = nc.dram_tensor("dde", [2, 128, 16], F32, kind="ExternalOutput").ap()
        dsl2 = nc.dram_tensor("dsl2", [2, 128, 128 * 24], F32, kind="ExternalOutput").ap()

    BD = D + 1
    oaw = max(bh * BD, 5 * 1024)

    es = ExitStack()
    with es:
        sb = lambda n, sh, dt: es.enter_context(nc.sbuf_tensor(n, sh, dt))
        qfm = sb("qfm", [128, EC * tok], BF16)
        oacc = sb("oacc", [128, oaw], F32)
        ekfm = [sb(f"ekfm{i}", [128, EC * GK], BF16) for i in range(2)]
        ektm = [sb(f"ektm{i}", [128, KC * D], BF16) for i in range(2)]
        pt = sb("pt", [128, max(KC * hw, 8192)], BF16)
        xkb = [sb(f"xkb{i}", [128, 4 * D], BF16) for i in range(2)]
        maskb = sb("maskb", [128, 2 * KC * 128], BF16)
        onesb = sb("onesb", [128, 1], BF16)
        onew = sb("onew", [128, 1], F32)
        nshb = sb("nshb", [128, 1], F32)
        epsb = sb("epsb", [128, 1], F32)
        c1024b = sb("c1024b", [128, 1], F32)
        slots = sb("slots", [128, 128 * 24], F32)
        brow = sb("brow", [1, D], F32)
        onesf = sb("onesf", [1, 128], F32)
        xpb = [sb(f"xpb{i}", [128, D], F32) for i in range(2)]
        xqr = [sb(f"xqr{i}", [128, D], F32) for i in range(2)]
        fm_st = sb("fm_st", [128, 4096], BF16)
        idb = sb("idb", [128, 128], BF16)
        idio = sb("idio", [128, 1], F32)

        s_ps = [es.enter_context(nc.psum_tensor(f"s_ps{i}", [128, 512], F32))
                for i in range(2)]
        av_e0 = [es.enter_context(nc.psum_tensor(f"av_e0_{i}", [128, 512], F32))
                 for i in range(2)]
        av_e1 = [es.enter_context(nc.psum_tensor(f"av_e1_{i}", [128, 512], F32))
                 for i in range(2)]
        av_dn = [es.enter_context(nc.psum_tensor("av_dn_0", [128, 512], F32))] * 2
        tr_ps = es.enter_context(nc.psum_tensor("tr_ps", [128, 1024], BF16))

        block = es.enter_context(nc.Block())
        p = Prog(nc, es)

        eqfm = [ektm[0][:, 0:4096], ektm[0][:, 4096:8192]]
        rwsb = ekfm[1]
        eqst = [pt[:, 0:1024], pt[:, 1024:2048]]
        ufm = [ekfm[0][:, 0:4096], ekfm[0][:, 4096:8192]]
        h1f = ekfm[1][:, 0:4096]
        h2f = ekfm[1][:, 4096:8192]
        ust = [ektm[0][:, 0:4096], ektm[0][:, 4096:8192]]
        w1s = [xkb[0], xkb[1]]
        if EC * tok >= 16384:
            w2s = qfm[:, 0:8192]
            w3s = qfm[:, 8192:16384]
        else:
            w2s = pt[:, 0:8192]
            w3s = ektm[1][:, 0:8192]
        thb = [oacc[:, 0:1024], oacc[:, 1024:2048]]
        brep = oacc[:, 2048:3072]
        outb = [oacc[:, 3072:4096], oacc[:, 4096:5120]]

        slot_n = [0]

        def slot():
            s = slot_n[0]
            slot_n[0] += 1
            assert s < 128, "stat slot overflow"
            return slots[:, s * 24:(s + 1) * 24]

        def raw_stats(srcs, waits):
            T = len(srcs)
            b0 = slot_n[0]
            slot_n[0] += T
            assert slot_n[0] <= 128, "stat slot overflow"
            q0 = b0 * 24
            raw = slots[:, q0:q0 + 12 * T]
            r3 = raw.rearrange("p (t c) -> p t c", t=T)

            def arr(j):
                return slots[:, q0 + 12 * T + j * T:q0 + 12 * T + (j + 1) * T]
            for k, src in enumerate(srcs):
                fl = slots[:, q0 + 12 * k:q0 + 12 * k + 12]
                p.op("vector", lambda e, o=fl, x=src: e.bn_stats(o[:, 0:6], x[:, 0:512]),
                     waits=(waits if k == 0 else []), drain=False)
                p.op("vector", lambda e, o=fl, x=src: e.bn_stats(o[:, 6:12], x[:, 512:1024]),
                     drain=False)
            mv_ = r3[:, :, 1:12:3]
            nv_ = r3[:, :, 2:12:3]
            sq_ = r3[:, :, 0:12:3]
            A, B, C, t_, B2, v1024 = arr(0), arr(1), arr(2), arr(3), arr(4), arr(5)
            p.op("vector", lambda e: e.tensor_mul(sq_, mv_, mv_))
            p.op("vector", lambda e: e.reduce_sum(A, sq_, axis=mybir.AxisListType.X))
            p.op("vector", lambda e: e.reduce_sum(B, mv_, axis=mybir.AxisListType.X))
            p.op("vector", lambda e: e.reduce_sum(C, nv_, axis=mybir.AxisListType.X))
            p.op("vector", lambda e: e.scalar_tensor_tensor(t_, A, 256.0, C, ALU.mult, ALU.add))
            p.op("vector", lambda e: e.tensor_mul(B2, B, B))
            ev = p.op("vector", lambda e: e.scalar_tensor_tensor(
                v1024, B2, -64.0, t_, ALU.mult, ALU.add), inc="ve")
            return arr, ev

        def ln_params(arr, ev):
            B, v1024, std, rstd, nmr = arr(1), arr(5), arr(6), arr(7), arr(8)
            evs = p.op("scalar", lambda e: e.activation(
                std, v1024, AF.Sqrt, bias=epsb[:, :], scale=c1024b[:, :]),
                waits=[ev], inc="sc")
            p.op("vector", lambda e: e.reciprocal(rstd, std), waits=[evs], drain=False)
            rdy = p.op("vector", lambda e: e.scalar_tensor_tensor(
                nmr, B, -0.25, rstd, ALU.mult, ALU.mult), inc="ve")
            return rdy

        def stat_batch(srcs, waits):
            arr, ev = raw_stats(srcs, waits)
            rdy = ln_params(arr, ev)
            rstd, nmr = arr(7), arr(8)
            return [(rstd[:, k:k + 1], nmr[:, k:k + 1])
                    for k in range(len(srcs))], rdy

        p.op("gpsimd", lambda e: e.memset(onesb[:, :], 1.0))
        p.op("gpsimd", lambda e: e.memset(onew[:, :], 1.0))
        p.op("gpsimd", lambda e: e.memset(nshb[:, :], -16.0))
        p.op("gpsimd", lambda e: e.memset(epsb[:, :], 1e-5))
        p.op("gpsimd", lambda e: e.memset(c1024b[:, :], 1.0 / 1024.0))
        ev_const = p.op("gpsimd", lambda e: e.memset(onesf[:, :], 1.0), inc="gp")
        idcol = slots[:, 2048:2176]
        evi0 = p.op("gpsimd", lambda e: e.iota(
            idcol, [[1, 128]], channel_multiplier=0,
            allow_small_or_imprecise_dtypes=True), inc="gp")
        evi1 = p.op("gpsimd", lambda e: e.iota(
            idio[:, :], [[1, 1]], channel_multiplier=1,
            allow_small_or_imprecise_dtypes=True), inc="gp")
        ev_idb = p.op("vector", lambda e: e.tensor_scalar(
            idb[:, :], idcol, idio[:, :], 1.0, ALU.is_equal, ALU.mult),
            waits=[evi0, evi1], inc="ve", drain=False)
        ev_mask = p.dma("sync", maskb.ap().rearrange("p (b c f) -> p b c f", b=2, c=KC),
                        mk.rearrange("(b c p) f -> p b c f", b=2, p=128), ch="mkl")
        ev_bias = p.dma("sync", brow[:, :], bi, ch="bil")
        ev_rw = p.dma("gpsimd", rwsb.ap().rearrange("p (dc e) -> p dc e", dc=EC),
                      rw.rearrange("(dc p) e -> p dc e", p=128), ch="wdr")

        eqw_ev = None
        xqr_last = [None, None]
        eqst_free = [None, None]
        trps_free = [None]
        fmst_free = [None]
        for tp in range(ntile // 2):
            evls = []
            for lane in range(2):
                t = 2 * tp + lane
                evls.append(p.dma("sync", xqr[lane][:, :], xq[t * 128:(t + 1) * 128, :],
                                  waits=[xqr_last[lane]], ch=f"xql{lane}"))
            prm, rdy = stat_batch([xqr[0], xqr[1]], evls)
            for lane in range(2):
                t = 2 * tp + lane
                rstd, nmr = prm[lane]
                evn = p.op("gpsimd", lambda e, o=eqst[lane], i=xqr[lane], r=rstd, n=nmr:
                           e.tensor_scalar(o[:, :], i[:, :], r, n, ALU.mult, ALU.add),
                           waits=[rdy, eqst_free[lane]], inc="gp")
                xqr_last[lane] = evn
                tev = None
                for ec in range(EC):
                    tev = p.op("tensor", lambda e, lane=lane, ec=ec:
                               e.transpose(tr_ps[:, ec * 128:(ec + 1) * 128],
                                           eqst[lane][:, ec * 128:(ec + 1) * 128],
                                           idb[:, :]),
                               waits=([evn, ev_idb, trps_free[0]] if ec == 0 else []),
                               inc="te")
                    if ec == 0:
                        pass
                fev = p.op("vector", lambda e: e.tensor_copy(fm_st[:, 0:1024], tr_ps[:, :]),
                           waits=[tev, fmst_free[0]], inc="ve", drain=False)
                trps_free[0] = fev
                fmd = p.dma("sync",
                            eq_fm.rearrange("(dc p) q -> p dc q", p=128)[:, :, t * 128:(t + 1) * 128],
                            fm_st[:, 0:1024].rearrange("p (dc q) -> p dc q", dc=EC),
                            waits=[fev], ch="efw")
                fmst_free[0] = fmd
                eqw_ev = p.dma("sync", eq_tm[t * 128:(t + 1) * 128, :], eqst[lane][:, :],
                               waits=[evn], ch="eqw")
                eqst_free[lane] = fmd
        eqw_all = eqw_ev
        efw_all = ("efw", 16 * ntile)

        eqf_last = [None, None]
        sps_q = [None, None]
        ev_qfm = None
        for qt in range(nqt):
            lane = qt % 2
            tdma_evs = [p.dma(
                "sync", eqfm[lane].rearrange("p (dc q) -> p dc q", dc=EC),
                eq_fm.rearrange("(dc p) q -> p dc q", p=128)[:, :, qt * 512:(qt + 1) * 512],
                waits=[("efw", 16 * min(4 * qt + 4, ntile)), eqf_last[lane]],
                ch=f"eqf{lane}")]
            last_mm = None
            for ec in range(EC):
                ps = (av_e0[0] if ec % 2 == 0 else av_e1[0])[:, 0:512]
                mm_ev = None
                for dc in range(EC):
                    wts = [tdma_evs[-1], ev_rw, sps_q[ec % 2]] if dc == 0 else []
                    mm_ev = p.op("tensor", lambda e, o=ps, dc=dc, ec=ec, lane=lane:
                                 e.matmul(o,
                                          rwsb[:, dc * 1024 + ec * 128:dc * 1024 + (ec + 1) * 128],
                                          eqfm[lane][:, dc * 512:(dc + 1) * 512],
                                          start=(dc == 0), stop=(dc == EC - 1)),
                                 waits=wts, inc="te")
                last_mm = mm_ev
                sps_q[ec % 2] = p.op("vector", lambda e, i=ps, qt=qt, ec=ec:
                                     e.tensor_copy(qfm[:, ec * tok + qt * 512:ec * tok + (qt + 1) * 512], i),
                                     waits=[mm_ev], inc="ve", drain=False)
            eqf_last[lane] = last_mm
            ev_qfm = sps_q[(EC - 1) % 2]
        ev_qfm_all = ev_qfm
        if dbg:
            p.dma("sync", dq, qfm[:, :], waits=[ev_qfm_all], ch="dbg")

        tmw_ev = {}
        tmf_ev = {}
        xkb_last = [None, None]
        xkb_tr_last = [None, None]
        p3_next = [0]

        def emit_p3(upto):
            while p3_next[0] <= min(upto, nhalfg - 1):
                w = p3_next[0]
                p3_next[0] += 1
                lane = w % 2
                evl = p.dma("gpsimd",
                            xkb[lane].ap().rearrange("p (t e) -> p t e", t=4),
                            xk[w * 512:(w + 1) * 512, :].rearrange("(t p) e -> p t e", p=128),
                            waits=[xkb_last[lane], xkb_tr_last[lane]], ch=f"xkc{lane}")
                prm, rdy = stat_batch([xkb[lane][:, tl * D:(tl + 1) * D]
                                       for tl in range(4)], [evl])
                evn = None
                for tl in range(4):
                    rstd, nmr = prm[tl]
                    evn = p.op("gpsimd", lambda e, b=xkb[lane], tl=tl, r=rstd, n=nmr:
                               e.tensor_scalar(b[:, tl * D:(tl + 1) * D],
                                               b[:, tl * D:(tl + 1) * D], r, n,
                                               ALU.mult, ALU.add),
                               waits=([rdy] if tl == 0 else []),
                               inc=("gp" if tl == 3 else None))
                tmw_ev[w] = p.dma("sync",
                                  ek_tm[w * 512:(w + 1) * 512, :].rearrange("(t p) e -> p t e", p=128),
                                  xkb[lane].ap().rearrange("p (t e) -> p t e", t=4),
                                  waits=[evn], ch=f"tmw{lane}")
                for tl in range(4):
                    tev = None
                    for ec in range(EC):
                        tev = p.op("tensor", lambda e, lane=lane, tl=tl, ec=ec:
                                   e.transpose(tr_ps[:, ec * 128:(ec + 1) * 128],
                                               xkb[lane][:, tl * D + ec * 128:tl * D + (ec + 1) * 128],
                                               idb[:, :]),
                                   waits=([evn, ev_idb, trps_free[0]] if ec == 0 else []),
                                   inc="te")
                    fev = p.op("vector", lambda e, tl=tl: e.tensor_copy(
                        fm_st[:, tl * 1024:(tl + 1) * 1024], tr_ps[:, :]),
                        waits=[tev] + ([fmst_free[0]] if tl == 0 else []),
                        inc="ve", drain=False)
                    trps_free[0] = fev
                    tmf_ev[w] = p.dma(
                        "sync",
                        ek_fm.rearrange("(dc p) k -> p dc k", p=128)
                        [:, :, w * 512 + tl * 128:w * 512 + (tl + 1) * 128],
                        fm_st.ap().rearrange("p (tl dc q) -> p tl dc q",
                                             tl=4, dc=EC)[:, tl],
                        waits=[fev], ch=f"tmf{lane}")
                fmst_free[0] = tmf_ev[w]
                xkb_tr_last[lane] = tev
                xkb_last[lane] = tmw_ev[w]

        emit_p3(1)

        rnd = 0
        qfm_free_ev = [None]
        ekfm_last = [None, None]
        ektm_last = [None, None]
        sunit = 0
        sunit_exp = {}
        trio_free = [None, None]
        dn_free = [None]
        avcnt = 0
        fin_all_ev = None
        xpw_ev = {}
        dbg_gate = [None]
        xqr_fin_last = list(xqr_last)
        xpb_last = [None, None]
        ptm = pt[:, 0:KC * hw].rearrange("p (c q) -> p c q", c=KC)
        mkm = maskb.ap().rearrange("p (b c f) -> p b c f", b=2, c=KC)

        for h in range(2):
            if h == 1:
                emit_p3(nhalfg - 1)
            glast = (h * bh + bh - 1) // 2
            av_last_ev = None
            for g in range(glast + 1):
                emit_p3(nhalfg - 1 if (h == 1 or g == glast) else 2 * g + 3)
                lane = rnd % 2
                wts0 = [tmf_ev[2 * g], tmf_ev[2 * g + 1], ekfm_last[lane]]
                if rnd <= 1:
                    wts0.append(ev_qfm_all)
                fm_evs = [p.dma(
                    "sync", ekfm[lane].ap().rearrange("p (dc k) -> p dc k", dc=EC),
                    ek_fm.rearrange("(dc p) k -> p dc k", p=128)[:, :, g * GK:(g + 1) * GK],
                    waits=wts0, ch=f"ekf{lane}")]
                tm_wts = [tmw_ev[2 * g], tmw_ev[2 * g + 1], ektm_last[lane]]
                if rnd == 0:
                    tm_wts.append(ev_qfm_all)
                tm_ev = p.dma("sync",
                              ektm[lane].ap().rearrange("p (kc e) -> p kc e", kc=KC),
                              ek_tm[g * GK:(g + 1) * GK, :].rearrange("(kc p) e -> p kc e", p=128),
                              waits=tm_wts, ch=f"ekt{lane}")

                cb = max(0, 256 * g - hw * h)
                csl = []
                c0 = cb
                while c0 < hw:
                    c1 = min((c0 // 512 + 1) * 512, hw)
                    csl.append((c0, c1))
                    c0 = c1
                grp_exp = None
                last_score_mm = None
                for c in range(KC):
                    for (a0, a1) in csl:
                        u = sunit
                        sunit += 1
                        ps = s_ps[u % 2]
                        wts = []
                        if c == 0 and a0 == csl[0][0]:
                            wts.append(fm_evs[-1])
                            wts.append(dbg_gate[0])
                        if u >= 2:
                            wts.append(sunit_exp[u - 2])
                        mm_ev = None
                        for ec in range(EC):
                            mm_ev = p.op("tensor", lambda e, ps=ps, lane=lane, c=c,
                                         ec=ec, a0=a0, a1=a1, h=h:
                                         e.matmul(ps[:, 0:a1 - a0],
                                                  ekfm[lane][:, ec * GK + c * 128:ec * GK + (c + 1) * 128],
                                                  qfm[:, ec * tok + h * hw + a0:ec * tok + h * hw + a1],
                                                  start=(ec == 0), stop=(ec == EC - 1)),
                                         waits=(wts if ec == 0 else []), inc="te")
                        last_score_mm = mm_ev
                        eev = p.op("scalar", lambda e, ps=ps, c=c, a0=a0, a1=a1:
                                   e.activation(pt[:, c * hw + a0:c * hw + a1],
                                                ps[:, 0:a1 - a0], AF.Exp,
                                                bias=nshb[:, :]),
                                   waits=[mm_ev], inc="sc")
                        sunit_exp[u] = eev
                        grp_exp = eev
                ekfm_last[lane] = last_score_mm
                qfm_free_ev[0] = last_score_mm
                mask_ev = None
                lbd = 2 * g - bh * h
                if 0 <= lbd < bh:
                    p.op("vector", lambda e, lbd=lbd: e.tensor_tensor(
                        ptm[:, :, lbd * 128:(lbd + 1) * 128],
                        ptm[:, :, lbd * 128:(lbd + 1) * 128],
                        mkm[:, 0], ALU.mult),
                        waits=[grp_exp, ev_mask], inc="ve", drain=False)
                    mask_ev = p.op("vector", lambda e, lbd=lbd: e.tensor_tensor(
                        ptm[:, :, (lbd + 1) * 128:(lbd + 2) * 128],
                        ptm[:, :, (lbd + 1) * 128:(lbd + 2) * 128],
                        mkm[:, 1], ALU.mult), inc="ve", drain=False)
                lbmin = max(0, 2 * g - bh * h)
                for lb in range(lbmin, bh):
                    par = avcnt % 2
                    wts = [grp_exp, tm_ev, mask_ev, trio_free[par]]
                    if h == 1 and g == 0 and lb == lbmin:
                        wts.append(fin_all_ev)
                        wts.append(dbg_gate[0])
                    for c in range(KC):
                        p.op("tensor", lambda e, par=par, c=c, lb=lb, lane=lane:
                             e.matmul(av_e0[par][:, 0:512],
                                      pt[:, c * hw + lb * 128:c * hw + (lb + 1) * 128],
                                      ektm[lane][:, c * D:c * D + 512],
                                      start=(c == 0), stop=(c == KC - 1)),
                             waits=(wts if c == 0 else []))
                    for c in range(KC):
                        p.op("tensor", lambda e, par=par, c=c, lb=lb, lane=lane:
                             e.matmul(av_e1[par][:, 0:512],
                                      pt[:, c * hw + lb * 128:c * hw + (lb + 1) * 128],
                                      ektm[lane][:, c * D + 512:(c + 1) * D],
                                      start=(c == 0), stop=(c == KC - 1)))
                    av_ev = None
                    for c in range(KC):
                        av_ev = p.op("tensor", lambda e, par=par, c=c, lb=lb:
                                     e.matmul(av_dn[par][:, 0:1],
                                              pt[:, c * hw + lb * 128:c * hw + (lb + 1) * 128],
                                              onesb[:, :],
                                              start=(c == 0), stop=(c == KC - 1)),
                                     waits=([dn_free[0]] if c == 0 else []),
                                     inc="te")
                    if g == 0:
                        p.op("vector", lambda e, par=par, lb=lb:
                             e.tensor_copy(oacc[:, lb * BD:lb * BD + 512], av_e0[par][:, :]),
                             waits=[av_ev], drain=False)
                        p.op("vector", lambda e, par=par, lb=lb:
                             e.tensor_copy(oacc[:, lb * BD + 512:lb * BD + 1024], av_e1[par][:, :]),
                             drain=False)
                        tf = p.op("vector", lambda e, par=par, lb=lb:
                                  e.tensor_copy(oacc[:, lb * BD + 1024:lb * BD + 1025], av_dn[par][:, 0:1]),
                                  inc="ve", drain=False)
                    else:
                        p.op("vector", lambda e, par=par, lb=lb:
                             e.tensor_add(oacc[:, lb * BD:lb * BD + 512], av_e0[par][:, :],
                                          oacc[:, lb * BD:lb * BD + 512]),
                             waits=[av_ev], drain=False)
                        p.op("vector", lambda e, par=par, lb=lb:
                             e.tensor_add(oacc[:, lb * BD + 512:lb * BD + 1024], av_e1[par][:, :],
                                          oacc[:, lb * BD + 512:lb * BD + 1024]),
                             drain=False)
                        tf = p.op("vector", lambda e, par=par, lb=lb:
                                  e.tensor_add(oacc[:, lb * BD + 1024:lb * BD + 1025], av_dn[par][:, 0:1],
                                               oacc[:, lb * BD + 1024:lb * BD + 1025]),
                                  inc="ve", drain=False)
                    trio_free[par] = tf
                    dn_free[0] = tf
                    av_last_ev = av_ev
                    avcnt += 1
                ektm_last[lane] = av_last_ev
                rnd += 1
            if dbg:
                dsl = slots[:, 128 * 24 - 16:128 * 24]
                dce = p.op("vector", lambda e, d=dsl: e.tensor_copy(
                    d[:, 0:bh], oacc[:, 1024:bh * BD:BD]), waits=[av_last_ev], inc="ve")
                p.dma("sync", dde[h][:, 0:bh], dsl[:, 0:bh], waits=[dce], ch="dbg")
                p.dma("sync", dpt[h], pt[:, 0:KC * hw], waits=[av_last_ev], ch="dbg")
                dbg_ev = p.dma("sync", doa[h],
                               oacc[:, 0:bh * BD].rearrange("p (b c) -> p b c", b=bh)[:, :, 0:1024],
                               waits=[dce], ch="dbg")
                dbg_gate[0] = dbg_ev
            arr, evc = raw_stats(
                [oacc[:, lb * BD:lb * BD + 1024] for lb in range(bh)],
                [trio_free[0], trio_free[1]])
            rdv, rr, vv = arr(0), arr(4), arr(2)
            stdv, g0v, sclv, nmbv = arr(6), arr(7), arr(3), arr(8)
            Bv, v1024 = arr(1), arr(5)
            p.op("vector", lambda e, o=rdv: e.reciprocal(o, oacc[:, 1024:bh * BD:BD]),
                 waits=[evc], drain=False)
            p.op("vector", lambda e, o=rr, a=rdv: e.tensor_mul(o, a, a))
            ev1 = p.op("vector", lambda e, o=vv, a=v1024, b=rr: e.tensor_mul(o, a, b),
                       inc="ve")
            ev2 = p.op("scalar", lambda e, o=stdv, a=vv: e.activation(
                o, a, AF.Sqrt, bias=epsb[:, :], scale=c1024b[:, :]),
                waits=[ev1], inc="sc")
            p.op("vector", lambda e, o=g0v, a=stdv: e.reciprocal(o, a), waits=[ev2],
                 drain=False)
            p.op("vector", lambda e, o=sclv, a=rdv, b=g0v: e.tensor_mul(o, a, b))
            ev3 = p.op("vector", lambda e, o=nmbv, a=Bv, b=sclv: e.scalar_tensor_tensor(
                o, a, -0.25, b, ALU.mult, ALU.mult), inc="ve")
            for lb in range(bh):
                i = h * bh + lb
                lane = lb % 2
                scl = sclv[:, lb:lb + 1]
                nmb = nmbv[:, lb:lb + 1]
                evx = p.dma("sync", xqr[lane][:, :], xq[i * 128:(i + 1) * 128, :],
                            waits=[xqr_fin_last[lane], av_last_ev], ch=f"xqf{lane}")
                ev4 = p.op("gpsimd", lambda e, b=xqr[lane], n=nmb:
                           e.tensor_scalar(b[:, :], b[:, :], onew[:, :], n,
                                           ALU.mult, ALU.add),
                           waits=[evx, ev3], inc="gp")
                ev5 = p.op("vector", lambda e, o=xpb[lane], lb=lb, s=scl, b=xqr[lane]:
                           e.scalar_tensor_tensor(o[:, :], oacc[:, lb * BD:lb * BD + 1024],
                                                  s, b[:, :], ALU.mult, ALU.add),
                           waits=[ev4, xpb_last[lane]], inc="xf", drain=False)
                xqr_fin_last[lane] = ev5
                evw = p.dma("sync", xp_d[i * 128:(i + 1) * 128, :], xpb[lane][:, :],
                            waits=[ev5], ch=f"xpw{i}")
                xpw_ev[i] = evw
                xpb_last[lane] = evw
                fin_all_ev = ev5
            if dbg:
                p.dma("sync", dsl2[h], slots[:, :], waits=[fin_all_ev], ch="dbg")


        evw1 = [p.dma("gpsimd", w1s[k].ap().rearrange("p (dc e) -> p dc e", dc=4),
                      w1[k * 512:(k + 1) * 512, :].rearrange("(dc p) e -> p dc e", p=128),
                      waits=[xkb_last[k]], ch=f"wd1{k}") for k in range(2)]
        wgate = qfm_free_ev[0] if EC * tok >= 16384 else fin_all_ev
        evw2 = p.dma("gpsimd", w2s.rearrange("p (dc e) -> p dc e", dc=EC),
                     w2.rearrange("(dc p) e -> p dc e", p=128),
                     waits=[wgate], ch="wd2")
        evw3 = p.dma("gpsimd", w3s.rearrange("p (dc e) -> p dc e", dc=EC),
                     w3.rearrange("(dc p) e -> p dc e", p=128),
                     waits=[wgate], ch="wd3")
        p.op("tensor", lambda e: e.matmul(av_e0[0][:, 0:512], onesf[:, :],
                                          brow[:, 0:512], start=True, stop=True),
             waits=[ev_bias, ev_const, trio_free[0], trio_free[1]], inc="te")
        evb2 = p.op("tensor", lambda e: e.matmul(av_e1[0][:, 0:512], onesf[:, :],
                                                 brow[:, 512:1024], start=True, stop=True),
                    inc="te")
        p.op("vector", lambda e: e.tensor_copy(brep[:, 0:512], av_e0[0][:, :]),
             waits=[evb2, fin_all_ev], inc="ve")
        evbr = p.op("vector", lambda e: e.tensor_copy(brep[:, 512:1024], av_e1[0][:, :]),
                    inc="ve")

        if EC * tok >= 16384:
            NB = 8
            attn_mm_done = [ektm_last[0], ektm_last[1]]
            xpl_bufs = [pt[:, 0:2048].bitcast(F32), pt[:, 2048:4096].bitcast(F32),
                        ektm[1][:, 0:2048].bitcast(F32),
                        ektm[1][:, 2048:4096].bitcast(F32),
                        xpb[0], xpb[1], xqr[0], xqr[1]]
            xpl_last = [list(attn_mm_done) for _ in range(4)] +                        [[xpb_last[0]], [xpb_last[1]],
                        [xqr_fin_last[0]], [xqr_fin_last[1]]]
        else:
            NB = 4
            xpl_bufs = [xpb[0], xpb[1], xqr[0], xqr[1]]
            xpl_last = [[xpb_last[0]], [xpb_last[1]],
                        [xqr_fin_last[0]], [xqr_fin_last[1]]]
        ust_free = [None, None]
        sps_f = [sunit_exp.get(sunit - 2), sunit_exp.get(sunit - 1)]
        sps_f = [sps_f[(sunit % 2)], sps_f[1 - (sunit % 2)]]
        s_ps3 = [s_ps[0], s_ps[1], av_dn[0]]
        sps_g = [sps_f[0], sps_f[1], dn_free[0]]
        trio_last = [evbr, fin_all_ev]
        outb_free = [None, None]
        thb_free = [None, None]
        outw_cnt = 0
        for st in range(nst):
            lane = st % 2
            evxs = []
            for tl in range(4):
                t = 4 * st + tl
                xl = xpl_bufs[t % NB]
                evxs.append(p.dma("sync", xl[:, :], xp_d[t * 128:(t + 1) * 128, :],
                                  waits=[xpw_ev[t]] + xpl_last[t % NB],
                                  ch=f"xpl{t % NB}"))
            prm, rdy = stat_batch([xpl_bufs[(4 * st + tl) % NB] for tl in range(4)],
                                  evxs)
            ust_evs = []
            for tl in range(4):
                t = 4 * st + tl
                xl = xpl_bufs[t % NB]
                rstd, nmr = prm[tl]
                evn = p.op("gpsimd", lambda e, o=ust[lane], tl=tl, i=xl, r=rstd, n=nmr:
                           e.tensor_scalar(o[:, tl * D:(tl + 1) * D], i[:, :], r, n,
                                           ALU.mult, ALU.add),
                           waits=[rdy] + ([ust_free[lane]] if tl == 0 else []),
                           inc="gp")
                ust_evs.append(evn)
            ufm_done = None
            for tl in range(4):
                tev = None
                for ec in range(EC):
                    tev = p.op("tensor", lambda e, lane=lane, tl=tl, ec=ec:
                               e.transpose(tr_ps[:, ec * 128:(ec + 1) * 128],
                                           ust[lane][:, tl * D + ec * 128:tl * D + (ec + 1) * 128],
                                           idb[:, :]),
                               waits=([ust_evs[tl], trps_free[0]] if ec == 0 else []),
                               inc="te")
                ufm_done = p.op("vector", lambda e, lane=lane, tl=tl: e.tensor_copy(
                    ufm[lane].rearrange("p (dc q) -> p dc q", dc=EC)[:, :, tl * 128:(tl + 1) * 128],
                    tr_ps[:, :].rearrange("p (dc q) -> p dc q", dc=EC)),
                    waits=[tev], inc="ve", drain=False)
                trps_free[0] = ufm_done
            ust_free[lane] = tev
            h_done = None
            for stage in range(2):
                last_mm_stage = None
                for ec in range(EC):
                    kg = 16 * st + 8 * stage + ec
                    ps = s_ps3[kg % 3]
                    mm_ev = None
                    for dc in range(EC):
                        wts = []
                        if dc == 0:
                            wts = [sps_g[kg % 3]]
                            if stage == 0:
                                wts += [ufm_done, evw1[0], evw1[1]]
                            else:
                                wts += [evw2]
                                if ec == 0:
                                    wts.append(h_done)
                        if stage == 0:
                            def bmm(e, ps=ps, dc=dc, ec=ec, lane=lane):
                                wsl = w1s[dc // 4][:, (dc % 4) * 1024 + ec * 128:(dc % 4) * 1024 + (ec + 1) * 128]
                                return e.matmul(ps[:, :], wsl,
                                                ufm[lane][:, dc * 512:(dc + 1) * 512],
                                                start=(dc == 0), stop=(dc == EC - 1))
                        else:
                            def bmm(e, ps=ps, dc=dc, ec=ec):
                                wsl = w2s[:, dc * 1024 + ec * 128:dc * 1024 + (ec + 1) * 128]
                                return e.matmul(ps[:, :], wsl,
                                                h1f[:, dc * 512:(dc + 1) * 512],
                                                start=(dc == 0), stop=(dc == EC - 1))
                        mm_ev = p.op("tensor", bmm, waits=wts, inc="te")
                    last_mm_stage = mm_ev
                    dstf = h1f if stage == 0 else h2f
                    if kg % 2 == 1:
                        cp = p.op("vector", lambda e, ps=ps, dstf=dstf, ec=ec:
                                  e.tensor_copy(dstf[:, ec * 512:(ec + 1) * 512], ps[:, :]),
                                  waits=[mm_ev], inc="ve", drain=False)
                        sps_g[kg % 3] = cp
                        sps_g[(kg - 1) % 3] = cp
                        last_cp = cp
                    else:
                        p.op("vector", lambda e, ps=ps, dstf=dstf, ec=ec:
                             e.tensor_copy(dstf[:, ec * 512:(ec + 1) * 512], ps[:, :]),
                             waits=[mm_ev], drain=False)
                h_done = last_cp
            h2_done = h_done
            for tb in range(4):
                t = 4 * st + tb
                mm_ev = None
                for eh in range(2):
                    bank = (av_e0 if eh == 0 else av_e1)[tb % 2]
                    for dc in range(EC):
                        wts = []
                        if eh == 0 and dc == 0:
                            wts = [h2_done, evw3, trio_last[tb % 2]]
                        def bmm3(e, bank=bank, dc=dc, eh=eh, tb=tb):
                            return e.matmul(bank[:, 0:512],
                                            h2f[:, dc * 512 + tb * 128:dc * 512 + (tb + 1) * 128],
                                            w3s[:, dc * 1024 + eh * 512:dc * 1024 + (eh + 1) * 512],
                                            start=(dc == 0), stop=(dc == EC - 1))
                        mm_ev = p.op("tensor", bmm3, waits=wts, inc="te")
                p.op("vector", lambda e, o=thb[tb % 2], tb=tb:
                     e.tensor_add(o[:, 0:512], av_e0[tb % 2][:, :], brep[:, 0:512]),
                     waits=[mm_ev, thb_free[tb % 2]], inc="ve", drain=False)
                evba = p.op("vector", lambda e, o=thb[tb % 2], tb=tb:
                            e.tensor_add(o[:, 512:1024], av_e1[tb % 2][:, :], brep[:, 512:1024]),
                            inc="ve", drain=False)
                trio_last[tb % 2] = evba
                evth = p.op("scalar", lambda e, o=thb[tb % 2]:
                            e.activation(o[:, :], o[:, :], AF.Tanh),
                            waits=[evba], inc="sc")
                prm4, rdy4 = stat_batch([thb[tb % 2]], [evth])
                rstd4, nmr4 = prm4[0]
                xl = xpl_bufs[t % NB]
                ev6 = p.op("gpsimd", lambda e, b=xl, n=nmr4:
                           e.tensor_scalar(b[:, :], b[:, :], onew[:, :], n,
                                           ALU.mult, ALU.add),
                           waits=[rdy4], inc="gp")
                ev7 = p.op("vector", lambda e, o=outb[tb % 2], i=thb[tb % 2], r=rstd4, b=xl:
                           e.scalar_tensor_tensor(o[:, :], i[:, :], r, b[:, :],
                                                  ALU.mult, ALU.add),
                           waits=[ev6, outb_free[tb % 2]], inc="ve", drain=False)
                thb_free[tb % 2] = ev7
                xpl_last[t % NB] = [ev7]
                evo = p.dma("scalar", out[t * 128:(t + 1) * 128, :], outb[tb % 2][:, :],
                            waits=[ev7], ch="outw")
                outw_cnt += 1
                outb_free[tb % 2] = evo
        p.op("scalar", lambda e: e.nop(), waits=[("outw", 16 * outw_cnt)])

        p.emit(block)
    return nc


def _shard_rows(tok, j):
    nb = tok // 128
    return np.concatenate([np.arange(128 * (4 * i + j), 128 * (4 * i + j) + 128)
                           for i in range(nb)])


def _mask_for(j):
    c = np.arange(KC)[:, None, None]
    pp = np.arange(128)[None, :, None]
    f = np.arange(128)[None, None, :]
    m0 = (128 * c + pp <= 128 * j + f)
    m1 = (128 * (c - 4) + pp <= 128 * j + f)
    m = np.stack([m0, m1]).astype(ml_dtypes.bfloat16)
    return m.reshape(2 * KC * 128, 128)


def _run(x, R, W1, W2, W3, B, seq):
    tok = seq // 4
    nc = build_nc(seq, tok)
    in_maps = []
    rows_all = []
    for c in range(8):
        b, j = c // 4, c % 4
        rows = _shard_rows(tok, j)
        rows_all.append(rows)
        in_maps.append({
            "xq": np.ascontiguousarray(x[b][rows]),
            "xk": np.ascontiguousarray(x[b]),
            "rw": R, "w1": W1, "w2": W2, "w3": W3,
            "bi": np.ascontiguousarray(B.reshape(1, D)),
            "mk": _mask_for(j),
        })
    res = run_bass_kernel_spmd(nc, in_maps, core_ids=list(range(8)))
    outx = np.empty((2, seq, D), np.float32)
    for c in range(8):
        outx[c // 4][rows_all[c]] = res.results[c]["out"]
    return outx


def kernel(L0_attn_encoding, L0_attn_R, L0_ffn_weight, L0_ffn_aux_weight1,
           L0_ffn_aux_weight2, L0_ffn_bias):
    x = np.ascontiguousarray(np.asarray(L0_attn_encoding, dtype=np.float32))
    R = np.ascontiguousarray(np.asarray(L0_attn_R, dtype=np.float32))
    W1 = np.ascontiguousarray(np.asarray(L0_ffn_weight, dtype=np.float32))
    W2 = np.ascontiguousarray(np.asarray(L0_ffn_aux_weight1, dtype=np.float32))
    W3 = np.ascontiguousarray(np.asarray(L0_ffn_aux_weight2, dtype=np.float32))
    B = np.ascontiguousarray(np.asarray(L0_ffn_bias, dtype=np.float32))
    return _run(x, R, W1, W2, W3, B, seq=x.shape[1])


# version 14 (final)
# speedup vs baseline: 1.0265x; 1.0265x over previous
import numpy as np
import ml_dtypes
from contextlib import ExitStack

import concourse.bass as bass
import concourse.mybir as mybir
from concourse.bass_utils import run_bass_kernel_spmd

F32 = mybir.dt.float32
BF16 = mybir.dt.bfloat16
AF = mybir.ActivationFunctionType
ALU = mybir.AluOpType

D = 1024
EC = 8
GK = 1024
KC = GK // 128

ENGINES = ("tensor", "vector", "scalar", "sync", "gpsimd")


class Prog:

    def __init__(self, nc, es):
        self.nc = nc
        self.es = es
        self.ops = []
        self.sems = {}
        self.cnt = {}

    def ch(self, name):
        if name not in self.sems:
            self.sems[name] = self.es.enter_context(self.nc.semaphore(name))
            self.cnt[name] = 0
        return name

    def op(self, eng, build, waits=(), inc=None, amt=1, drain=True):
        waits = tuple((c, v) for w in waits if w is not None for (c, v) in [w])
        for c, _ in waits:
            self.ch(c)
        ev = None
        if inc is not None:
            self.ch(inc)
            self.cnt[inc] += amt
            ev = (inc, self.cnt[inc])
        self.ops.append(dict(eng=eng, build=build, waits=waits,
                             inc=(inc, amt) if inc else None, is_dma=(amt == 16),
                             drain=(drain and eng == "vector")))
        return ev

    def dma(self, eng, out, in_, waits=(), ch=None, transpose=False):
        if transpose:
            def build(e, out=out, in_=in_):
                return e.dma_start_transpose(out=out, in_=in_)
        else:
            def build(e, out=out, in_=in_):
                return e.dma_start(out=out, in_=in_)
        return self.op(eng, build, waits=waits, inc=ch, amt=16)

    def _batch_incs(self):
        waited = {}
        for o in self.ops:
            for c, v in o["waits"]:
                waited.setdefault(c, set()).add(v)
        inc_engs = {}
        for o in self.ops:
            if o["inc"]:
                inc_engs.setdefault(o["inc"][0], set()).add(o["eng"])
        cum = {}
        pend = {}
        last_keep = {}
        for o in self.ops:
            if not o["inc"] or o["is_dma"] or o["eng"] == "tensor" \
                    or o["eng"] == "sync":
                continue
            c, a = o["inc"]
            if len(inc_engs[c]) > 1:
                continue
            lo = cum.get(c, 0)
            hi = lo + a
            cum[c] = hi
            if any(lo < t <= hi for t in waited.get(c, ())):
                o["inc"] = (c, a + pend.pop(c, 0))
                last_keep[c] = o
            else:
                pend[c] = pend.get(c, 0) + a
                o["inc"] = None
        for c, a in pend.items():
            if c in last_keep:
                ch, aa = last_keep[c]["inc"]
                last_keep[c]["inc"] = (ch, aa + a)

    def check_deadlock(self):
        q = {e: [o for o in self.ops if o["eng"] == e] for e in ENGINES}
        pos = {e: 0 for e in ENGINES}
        val = {c: 0 for c in self.cnt}
        progress = True
        while progress:
            progress = False
            for e in ENGINES:
                while pos[e] < len(q[e]):
                    o = q[e][pos[e]]
                    if all(val[c] >= v for c, v in o["waits"]):
                        if o["inc"]:
                            c, a = o["inc"]
                            val[c] += a
                        pos[e] += 1
                        progress = True
                    else:
                        break
        if any(pos[e] < len(q[e]) for e in ENGINES):
            msg = []
            for e in ENGINES:
                if pos[e] < len(q[e]):
                    o = q[e][pos[e]]
                    bad = [(c, v, val[c]) for c, v in o["waits"] if val[c] < v]
                    msg.append(f"{e}@{pos[e]}/{len(q[e])} needs {bad}")
            raise RuntimeError("deadlock: " + "; ".join(msg))

    def emit(self, block):
        self._batch_incs()
        self.check_deadlock()
        sems = self.sems
        for e in ENGINES:
            myops = [o for o in self.ops if o["eng"] == e]

            def body(eng, e=e, myops=myops):
                clean = False
                for o in myops:
                    for c, v in o["waits"]:
                        eng.wait_ge(sems[c], v)
                    if o.get("drain") and not clean:
                        eng.drain()
                    ins = o["build"](eng)
                    clean = False
                    if o["inc"]:
                        c, a = o["inc"]
                        if e in ("vector", "scalar", "gpsimd") and not o["is_dma"]:
                            eng.drain().then_inc(sems[c], a)
                            clean = True
                        else:
                            ins.then_inc(sems[c], a)
            getattr(block, e)(body)


def build_nc(seq, tok, dbg=False):
    nb = tok // 128
    bh = nb // 2
    ng = seq // GK
    hw = bh * 128
    nst = tok // 512
    nqt = tok // 512
    ntile = tok // 128
    nhalfg = seq // 512

    nc = bass.Bass("TRN2", target_bir_lowering=False)

    xq = nc.dram_tensor("xq", [tok, D], F32, kind="ExternalInput").ap()
    xk = nc.dram_tensor("xk", [seq, D], F32, kind="ExternalInput").ap()
    rw = nc.dram_tensor("rw", [D, D], F32, kind="ExternalInput").ap()
    w1 = nc.dram_tensor("w1", [D, D], F32, kind="ExternalInput").ap()
    w2 = nc.dram_tensor("w2", [D, D], F32, kind="ExternalInput").ap()
    w3 = nc.dram_tensor("w3", [D, D], F32, kind="ExternalInput").ap()
    bi = nc.dram_tensor("bi", [1, D], F32, kind="ExternalInput").ap()
    mk = nc.dram_tensor("mk", [2 * KC * 128, 128], BF16, kind="ExternalInput").ap()
    out = nc.dram_tensor("out", [tok, D], F32, kind="ExternalOutput").ap()

    sk = "ExternalOutput" if dbg else "Internal"
    ek_tm = nc.dram_tensor("ek_tm", [seq, D], BF16, kind=sk).ap()
    ek_fm = nc.dram_tensor("ek_fm", [EC * 128, seq], BF16, kind="Internal").ap()
    eq_fm = nc.dram_tensor("eq_fm", [EC * 128, tok], BF16, kind="Internal").ap()
    eq_tm = nc.dram_tensor("eq_tm", [tok, D], BF16, kind=sk).ap()
    u_tm = nc.dram_tensor("u_tm", [tok, D], BF16, kind=sk).ap()
    xp_d = nc.dram_tensor("xp_d", [tok, D], F32, kind=sk).ap()
    if dbg:
        dq = nc.dram_tensor("dq", [128, EC * tok], BF16, kind="ExternalOutput").ap()
        dpt = nc.dram_tensor("dpt", [2, 128, KC * (tok // 2)], BF16, kind="ExternalOutput").ap()
        doa = nc.dram_tensor("doa", [2, 128, (tok // 2) * 8], F32, kind="ExternalOutput").ap()
        dde = nc.dram_tensor("dde", [2, 128, 16], F32, kind="ExternalOutput").ap()
        dsl2 = nc.dram_tensor("dsl2", [2, 128, 128 * 24], F32, kind="ExternalOutput").ap()

    BD = D + 1
    oaw = max(bh * BD, 5 * 1024)

    es = ExitStack()
    with es:
        sb = lambda n, sh, dt: es.enter_context(nc.sbuf_tensor(n, sh, dt))
        qfm = sb("qfm", [128, EC * tok], BF16)
        oacc = sb("oacc", [128, oaw], F32)
        ekfm = [sb(f"ekfm{i}", [128, EC * GK], BF16) for i in range(2)]
        ektm = [sb(f"ektm{i}", [128, KC * D], BF16) for i in range(2)]
        pt = sb("pt", [128, max(KC * hw, 8192)], BF16)
        xkb = [sb(f"xkb{i}", [128, 4 * D], BF16) for i in range(2)]
        maskb = sb("maskb", [128, 2 * KC * 128], BF16)
        onesb = sb("onesb", [128, 1], BF16)
        onew = sb("onew", [128, 1], F32)
        nshb = sb("nshb", [128, 1], F32)
        epsb = sb("epsb", [128, 1], F32)
        c1024b = sb("c1024b", [128, 1], F32)
        slots = sb("slots", [128, 128 * 24], F32)
        brow = sb("brow", [1, D], F32)
        onesf = sb("onesf", [1, 128], F32)
        xpb = [sb(f"xpb{i}", [128, D], F32) for i in range(2)]
        xqr = [sb(f"xqr{i}", [128, D], F32) for i in range(2)]
        fm_st = sb("fm_st", [128, 4096], BF16)
        idb = sb("idb", [128, 128], BF16)
        idio = sb("idio", [128, 1], F32)

        s_ps = [es.enter_context(nc.psum_tensor(f"s_ps{i}", [128, 512], F32))
                for i in range(2)]
        av_e0 = [es.enter_context(nc.psum_tensor(f"av_e0_{i}", [128, 512], F32))
                 for i in range(2)]
        av_e1 = [es.enter_context(nc.psum_tensor(f"av_e1_{i}", [128, 512], F32))
                 for i in range(2)]
        av_dn = [es.enter_context(nc.psum_tensor("av_dn_0", [128, 512], F32))] * 2
        tr_ps = es.enter_context(nc.psum_tensor("tr_ps", [128, 1024], BF16))

        block = es.enter_context(nc.Block())
        p = Prog(nc, es)

        eqfm = [ektm[0][:, 0:4096], ektm[0][:, 4096:8192]]
        rwsb = ekfm[1]
        eqst = [pt[:, 0:1024], pt[:, 1024:2048]]
        ufm = [ekfm[0][:, 0:4096], ekfm[0][:, 4096:8192]]
        h1f = ekfm[1][:, 0:4096]
        h2f = ekfm[1][:, 4096:8192]
        ust = [ektm[0][:, 0:4096], ektm[0][:, 4096:8192]]
        w1s = [xkb[0], xkb[1]]
        if EC * tok >= 16384:
            w2s = qfm[:, 0:8192]
            w3s = qfm[:, 8192:16384]
        else:
            w2s = pt[:, 0:8192]
            w3s = ektm[1][:, 0:8192]
        thb = [oacc[:, 0:1024], oacc[:, 1024:2048]]
        brep = oacc[:, 2048:3072]
        outb = [oacc[:, 3072:4096], oacc[:, 4096:5120]]

        slot_n = [0]

        def slot():
            s = slot_n[0]
            slot_n[0] += 1
            assert s < 128, "stat slot overflow"
            return slots[:, s * 24:(s + 1) * 24]

        def raw_stats(srcs, waits):
            T = len(srcs)
            b0 = slot_n[0]
            slot_n[0] += T
            assert slot_n[0] <= 128, "stat slot overflow"
            q0 = b0 * 24
            raw = slots[:, q0:q0 + 12 * T]
            r3 = raw.rearrange("p (t c) -> p t c", t=T)

            def arr(j):
                return slots[:, q0 + 12 * T + j * T:q0 + 12 * T + (j + 1) * T]
            for k, src in enumerate(srcs):
                fl = slots[:, q0 + 12 * k:q0 + 12 * k + 12]
                p.op("vector", lambda e, o=fl, x=src: e.bn_stats(o[:, 0:6], x[:, 0:512]),
                     waits=(waits if k == 0 else []), drain=False)
                p.op("vector", lambda e, o=fl, x=src: e.bn_stats(o[:, 6:12], x[:, 512:1024]),
                     drain=False)
            mv_ = r3[:, :, 1:12:3]
            nv_ = r3[:, :, 2:12:3]
            sq_ = r3[:, :, 0:12:3]
            A, B, C, t_, B2, v1024 = arr(0), arr(1), arr(2), arr(3), arr(4), arr(5)
            p.op("vector", lambda e: e.tensor_mul(sq_, mv_, mv_))
            p.op("vector", lambda e: e.reduce_sum(A, sq_, axis=mybir.AxisListType.X))
            p.op("vector", lambda e: e.reduce_sum(B, mv_, axis=mybir.AxisListType.X))
            p.op("vector", lambda e: e.reduce_sum(C, nv_, axis=mybir.AxisListType.X))
            p.op("vector", lambda e: e.scalar_tensor_tensor(t_, A, 256.0, C, ALU.mult, ALU.add))
            p.op("vector", lambda e: e.tensor_mul(B2, B, B))
            ev = p.op("vector", lambda e: e.scalar_tensor_tensor(
                v1024, B2, -64.0, t_, ALU.mult, ALU.add), inc="ve")
            return arr, ev

        def ln_params(arr, ev):
            B, v1024, std, rstd, nmr = arr(1), arr(5), arr(6), arr(7), arr(8)
            evs = p.op("scalar", lambda e: e.activation(
                std, v1024, AF.Sqrt, bias=epsb[:, :], scale=c1024b[:, :]),
                waits=[ev], inc="sc")
            p.op("vector", lambda e: e.reciprocal(rstd, std), waits=[evs], drain=False)
            rdy = p.op("vector", lambda e: e.scalar_tensor_tensor(
                nmr, B, -0.25, rstd, ALU.mult, ALU.mult), inc="ve")
            return rdy

        def stat_batch(srcs, waits):
            arr, ev = raw_stats(srcs, waits)
            rdy = ln_params(arr, ev)
            rstd, nmr = arr(7), arr(8)
            return [(rstd[:, k:k + 1], nmr[:, k:k + 1])
                    for k in range(len(srcs))], rdy

        p.op("gpsimd", lambda e: e.memset(onesb[:, :], 1.0))
        p.op("gpsimd", lambda e: e.memset(onew[:, :], 1.0))
        p.op("gpsimd", lambda e: e.memset(nshb[:, :], -16.0))
        p.op("gpsimd", lambda e: e.memset(epsb[:, :], 1e-5))
        p.op("gpsimd", lambda e: e.memset(c1024b[:, :], 1.0 / 1024.0))
        ev_const = p.op("gpsimd", lambda e: e.memset(onesf[:, :], 1.0), inc="gp")
        idcol = slots[:, 2048:2176]
        evi0 = p.op("gpsimd", lambda e: e.iota(
            idcol, [[1, 128]], channel_multiplier=0,
            allow_small_or_imprecise_dtypes=True), inc="gp")
        evi1 = p.op("gpsimd", lambda e: e.iota(
            idio[:, :], [[1, 1]], channel_multiplier=1,
            allow_small_or_imprecise_dtypes=True), inc="gp")
        ev_idb = p.op("vector", lambda e: e.tensor_scalar(
            idb[:, :], idcol, idio[:, :], 1.0, ALU.is_equal, ALU.mult),
            waits=[evi0, evi1], inc="ve", drain=False)
        ev_mask = p.dma("sync", maskb.ap().rearrange("p (b c f) -> p b c f", b=2, c=KC),
                        mk.rearrange("(b c p) f -> p b c f", b=2, p=128), ch="mkl")
        ev_bias = p.dma("sync", brow[:, :], bi, ch="bil")
        ev_rw = p.dma("gpsimd", rwsb.ap().rearrange("p (dc e) -> p dc e", dc=EC),
                      rw.rearrange("(dc p) e -> p dc e", p=128), ch="wdr")

        eqw_ev = None
        xql_bufs = [xqr[0], xqr[1], xpb[0], xpb[1]]
        xql_last = [None, None, None, None]
        eqst_free = [None, None]
        trps_free = [None]
        fmst_free = [None]
        for tp in range(ntile // 2):
            evls = []
            for lane in range(2):
                t = 2 * tp + lane
                evls.append(p.dma("sync", xql_bufs[t % 4][:, :],
                                  xq[t * 128:(t + 1) * 128, :],
                                  waits=[xql_last[t % 4]], ch=f"xql{t % 4}"))
            prm, rdy = stat_batch([xql_bufs[(2 * tp) % 4], xql_bufs[(2 * tp + 1) % 4]],
                                  evls)
            for lane in range(2):
                t = 2 * tp + lane
                rstd, nmr = prm[lane]
                evn = p.op("gpsimd", lambda e, o=eqst[lane], i=xql_bufs[t % 4],
                           r=rstd, n=nmr:
                           e.tensor_scalar(o[:, :], i[:, :], r, n, ALU.mult, ALU.add),
                           waits=[rdy, eqst_free[lane]], inc="gp")
                xql_last[t % 4] = evn
                tev = None
                for ec in range(EC):
                    tev = p.op("tensor", lambda e, lane=lane, ec=ec:
                               e.transpose(tr_ps[:, ec * 128:(ec + 1) * 128],
                                           eqst[lane][:, ec * 128:(ec + 1) * 128],
                                           idb[:, :]),
                               waits=([evn, ev_idb, trps_free[0]] if ec == 0 else []),
                               inc="te")
                    if ec == 0:
                        pass
                fev = p.op("vector", lambda e: e.tensor_copy(fm_st[:, 0:1024], tr_ps[:, :]),
                           waits=[tev, fmst_free[0]], inc="ve", drain=False)
                trps_free[0] = fev
                fmd = p.dma("scalar",
                            eq_fm.rearrange("(dc p) q -> p dc q", p=128)[:, :, t * 128:(t + 1) * 128],
                            fm_st[:, 0:1024].rearrange("p (dc q) -> p dc q", dc=EC),
                            waits=[fev], ch="efw")
                fmst_free[0] = fmd
                eqw_ev = p.dma("scalar", eq_tm[t * 128:(t + 1) * 128, :], eqst[lane][:, :],
                               waits=[evn], ch="eqw")
                eqst_free[lane] = fmd
        eqw_all = eqw_ev
        efw_all = ("efw", 16 * ntile)

        eqf_last = [None, None]
        sps_q = [None, None]
        ev_qfm = None
        for qt in range(nqt):
            lane = qt % 2
            tdma_evs = [p.dma(
                "sync", eqfm[lane].rearrange("p (dc q) -> p dc q", dc=EC),
                eq_fm.rearrange("(dc p) q -> p dc q", p=128)[:, :, qt * 512:(qt + 1) * 512],
                waits=[("efw", 16 * min(4 * qt + 4, ntile)), eqf_last[lane]],
                ch=f"eqf{lane}")]
            last_mm = None
            for ec in range(EC):
                ps = (av_e0[0] if ec % 2 == 0 else av_e1[0])[:, 0:512]
                mm_ev = None
                for dc in range(EC):
                    wts = [tdma_evs[-1], ev_rw, sps_q[ec % 2]] if dc == 0 else []
                    mm_ev = p.op("tensor", lambda e, o=ps, dc=dc, ec=ec, lane=lane:
                                 e.matmul(o,
                                          rwsb[:, dc * 1024 + ec * 128:dc * 1024 + (ec + 1) * 128],
                                          eqfm[lane][:, dc * 512:(dc + 1) * 512],
                                          start=(dc == 0), stop=(dc == EC - 1)),
                                 waits=wts, inc="te")
                last_mm = mm_ev
                sps_q[ec % 2] = p.op("vector", lambda e, i=ps, qt=qt, ec=ec:
                                     e.tensor_copy(qfm[:, ec * tok + qt * 512:ec * tok + (qt + 1) * 512], i),
                                     waits=[mm_ev], inc="ve", drain=False)
            eqf_last[lane] = last_mm
            ev_qfm = sps_q[(EC - 1) % 2]
        ev_qfm_all = ev_qfm
        if dbg:
            p.dma("sync", dq, qfm[:, :], waits=[ev_qfm_all], ch="dbg")

        tmw_ev = {}
        tmf_ev = {}
        xkb_last = [None, None]
        xkb_tr_last = [None, None]
        p3_next = [0]

        def emit_p3(upto):
            while p3_next[0] <= min(upto, nhalfg - 1):
                w = p3_next[0]
                p3_next[0] += 1
                lane = w % 2
                evl = p.dma("gpsimd",
                            xkb[lane].ap().rearrange("p (t e) -> p t e", t=4),
                            xk[w * 512:(w + 1) * 512, :].rearrange("(t p) e -> p t e", p=128),
                            waits=[xkb_last[lane], xkb_tr_last[lane]], ch=f"xkc{lane}")
                prm, rdy = stat_batch([xkb[lane][:, tl * D:(tl + 1) * D]
                                       for tl in range(4)], [evl])
                evn = None
                for tl in range(4):
                    rstd, nmr = prm[tl]
                    evn = p.op("gpsimd", lambda e, b=xkb[lane], tl=tl, r=rstd, n=nmr:
                               e.tensor_scalar(b[:, tl * D:(tl + 1) * D],
                                               b[:, tl * D:(tl + 1) * D], r, n,
                                               ALU.mult, ALU.add),
                               waits=([rdy] if tl == 0 else []),
                               inc=("gp" if tl == 3 else None))
                tmw_ev[w] = p.dma("sync",
                                  ek_tm[w * 512:(w + 1) * 512, :].rearrange("(t p) e -> p t e", p=128),
                                  xkb[lane].ap().rearrange("p (t e) -> p t e", t=4),
                                  waits=[evn], ch=f"tmw{lane}")
                for tl in range(4):
                    tev = None
                    for ec in range(EC):
                        tev = p.op("tensor", lambda e, lane=lane, tl=tl, ec=ec:
                                   e.transpose(tr_ps[:, ec * 128:(ec + 1) * 128],
                                               xkb[lane][:, tl * D + ec * 128:tl * D + (ec + 1) * 128],
                                               idb[:, :]),
                                   waits=([evn, ev_idb, trps_free[0]] if ec == 0 else []),
                                   inc="te")
                    fev = p.op("vector", lambda e, tl=tl: e.tensor_copy(
                        fm_st[:, tl * 1024:(tl + 1) * 1024], tr_ps[:, :]),
                        waits=[tev] + ([fmst_free[0]] if tl == 0 else []),
                        inc="ve", drain=False)
                    trps_free[0] = fev
                    tmf_ev[w] = p.dma(
                        "sync",
                        ek_fm.rearrange("(dc p) k -> p dc k", p=128)
                        [:, :, w * 512 + tl * 128:w * 512 + (tl + 1) * 128],
                        fm_st.ap().rearrange("p (tl dc q) -> p tl dc q",
                                             tl=4, dc=EC)[:, tl],
                        waits=[fev], ch=f"tmf{lane}")
                fmst_free[0] = tmf_ev[w]
                xkb_tr_last[lane] = tev
                xkb_last[lane] = tmw_ev[w]

        emit_p3(1)

        rnd = 0
        qfm_free_ev = [None]
        ekfm_last = [None, None]
        ektm_last = [None, None]
        sunit = 0
        sunit_exp = {}
        trio_free = [None, None]
        dn_free = [None]
        avcnt = 0
        fin_all_ev = None
        xpw_ev = {}
        dbg_gate = [None]
        xqr_fin_last = [xql_last[0], xql_last[1]]
        xpb_last = [xql_last[2], xql_last[3]]
        ptm = pt[:, 0:KC * hw].rearrange("p (c q) -> p c q", c=KC)
        mkm = maskb.ap().rearrange("p (b c f) -> p b c f", b=2, c=KC)

        for h in range(2):
            if h == 1:
                emit_p3(nhalfg - 1)
            glast = (h * bh + bh - 1) // 2
            av_last_ev = None
            for g in range(glast + 1):
                emit_p3(nhalfg - 1 if (h == 1 or g == glast) else 2 * g + 3)
                lane = rnd % 2
                wts0 = [tmf_ev[2 * g], tmf_ev[2 * g + 1], ekfm_last[lane]]
                if rnd <= 1:
                    wts0.append(ev_qfm_all)
                fm_evs = [p.dma(
                    "sync", ekfm[lane].ap().rearrange("p (dc k) -> p dc k", dc=EC),
                    ek_fm.rearrange("(dc p) k -> p dc k", p=128)[:, :, g * GK:(g + 1) * GK],
                    waits=wts0, ch=f"ekf{lane}")]
                tm_wts = [tmw_ev[2 * g], tmw_ev[2 * g + 1], ektm_last[lane]]
                if rnd == 0:
                    tm_wts.append(ev_qfm_all)
                tm_ev = p.dma("sync",
                              ektm[lane].ap().rearrange("p (kc e) -> p kc e", kc=KC),
                              ek_tm[g * GK:(g + 1) * GK, :].rearrange("(kc p) e -> p kc e", p=128),
                              waits=tm_wts, ch=f"ekt{lane}")

                cb = max(0, 256 * g - hw * h)
                csl = []
                c0 = cb
                while c0 < hw:
                    c1 = min((c0 // 512 + 1) * 512, hw)
                    csl.append((c0, c1))
                    c0 = c1
                grp_exp = None
                last_score_mm = None
                for c in range(KC):
                    for (a0, a1) in csl:
                        u = sunit
                        sunit += 1
                        ps = s_ps[u % 2]
                        wts = []
                        if c == 0 and a0 == csl[0][0]:
                            wts.append(fm_evs[-1])
                            wts.append(dbg_gate[0])
                        if u >= 2:
                            wts.append(sunit_exp[u - 2])
                        mm_ev = None
                        for ec in range(EC):
                            mm_ev = p.op("tensor", lambda e, ps=ps, lane=lane, c=c,
                                         ec=ec, a0=a0, a1=a1, h=h:
                                         e.matmul(ps[:, 0:a1 - a0],
                                                  ekfm[lane][:, ec * GK + c * 128:ec * GK + (c + 1) * 128],
                                                  qfm[:, ec * tok + h * hw + a0:ec * tok + h * hw + a1],
                                                  start=(ec == 0), stop=(ec == EC - 1)),
                                         waits=(wts if ec == 0 else []), inc="te")
                        last_score_mm = mm_ev
                        eev = p.op("scalar", lambda e, ps=ps, c=c, a0=a0, a1=a1:
                                   e.activation(pt[:, c * hw + a0:c * hw + a1],
                                                ps[:, 0:a1 - a0], AF.Exp,
                                                bias=nshb[:, :]),
                                   waits=[mm_ev], inc="sc")
                        sunit_exp[u] = eev
                        grp_exp = eev
                ekfm_last[lane] = last_score_mm
                qfm_free_ev[0] = last_score_mm
                mask_ev = None
                lbd = 2 * g - bh * h
                if 0 <= lbd < bh:
                    p.op("vector", lambda e, lbd=lbd: e.tensor_tensor(
                        ptm[:, :, lbd * 128:(lbd + 1) * 128],
                        ptm[:, :, lbd * 128:(lbd + 1) * 128],
                        mkm[:, 0], ALU.mult),
                        waits=[grp_exp, ev_mask], inc="ve", drain=False)
                    mask_ev = p.op("vector", lambda e, lbd=lbd: e.tensor_tensor(
                        ptm[:, :, (lbd + 1) * 128:(lbd + 2) * 128],
                        ptm[:, :, (lbd + 1) * 128:(lbd + 2) * 128],
                        mkm[:, 1], ALU.mult), inc="ve", drain=False)
                lbmin = max(0, 2 * g - bh * h)
                for lb in range(lbmin, bh):
                    par = avcnt % 2
                    wts = [grp_exp, tm_ev, mask_ev, trio_free[par]]
                    if h == 1 and g == 0 and lb == lbmin:
                        wts.append(fin_all_ev)
                        wts.append(dbg_gate[0])
                    for c in range(KC):
                        p.op("tensor", lambda e, par=par, c=c, lb=lb, lane=lane:
                             e.matmul(av_e0[par][:, 0:512],
                                      pt[:, c * hw + lb * 128:c * hw + (lb + 1) * 128],
                                      ektm[lane][:, c * D:c * D + 512],
                                      start=(c == 0), stop=(c == KC - 1)),
                             waits=(wts if c == 0 else []))
                    for c in range(KC):
                        p.op("tensor", lambda e, par=par, c=c, lb=lb, lane=lane:
                             e.matmul(av_e1[par][:, 0:512],
                                      pt[:, c * hw + lb * 128:c * hw + (lb + 1) * 128],
                                      ektm[lane][:, c * D + 512:(c + 1) * D],
                                      start=(c == 0), stop=(c == KC - 1)))
                    av_ev = None
                    for c in range(KC):
                        av_ev = p.op("tensor", lambda e, par=par, c=c, lb=lb:
                                     e.matmul(av_dn[par][:, 0:1],
                                              pt[:, c * hw + lb * 128:c * hw + (lb + 1) * 128],
                                              onesb[:, :],
                                              start=(c == 0), stop=(c == KC - 1)),
                                     waits=([dn_free[0]] if c == 0 else []),
                                     inc="te")
                    if g == 0:
                        p.op("vector", lambda e, par=par, lb=lb:
                             e.tensor_copy(oacc[:, lb * BD:lb * BD + 512], av_e0[par][:, :]),
                             waits=[av_ev], drain=False)
                        p.op("vector", lambda e, par=par, lb=lb:
                             e.tensor_copy(oacc[:, lb * BD + 512:lb * BD + 1024], av_e1[par][:, :]),
                             drain=False)
                        tf = p.op("vector", lambda e, par=par, lb=lb:
                                  e.tensor_copy(oacc[:, lb * BD + 1024:lb * BD + 1025], av_dn[par][:, 0:1]),
                                  inc="ve", drain=False)
                    else:
                        p.op("vector", lambda e, par=par, lb=lb:
                             e.tensor_add(oacc[:, lb * BD:lb * BD + 512], av_e0[par][:, :],
                                          oacc[:, lb * BD:lb * BD + 512]),
                             waits=[av_ev], drain=False)
                        p.op("vector", lambda e, par=par, lb=lb:
                             e.tensor_add(oacc[:, lb * BD + 512:lb * BD + 1024], av_e1[par][:, :],
                                          oacc[:, lb * BD + 512:lb * BD + 1024]),
                             drain=False)
                        tf = p.op("vector", lambda e, par=par, lb=lb:
                                  e.tensor_add(oacc[:, lb * BD + 1024:lb * BD + 1025], av_dn[par][:, 0:1],
                                               oacc[:, lb * BD + 1024:lb * BD + 1025]),
                                  inc="ve", drain=False)
                    trio_free[par] = tf
                    dn_free[0] = tf
                    av_last_ev = av_ev
                    avcnt += 1
                ektm_last[lane] = av_last_ev
                rnd += 1
            if dbg:
                dsl = slots[:, 128 * 24 - 16:128 * 24]
                dce = p.op("vector", lambda e, d=dsl: e.tensor_copy(
                    d[:, 0:bh], oacc[:, 1024:bh * BD:BD]), waits=[av_last_ev], inc="ve")
                p.dma("sync", dde[h][:, 0:bh], dsl[:, 0:bh], waits=[dce], ch="dbg")
                p.dma("sync", dpt[h], pt[:, 0:KC * hw], waits=[av_last_ev], ch="dbg")
                dbg_ev = p.dma("sync", doa[h],
                               oacc[:, 0:bh * BD].rearrange("p (b c) -> p b c", b=bh)[:, :, 0:1024],
                               waits=[dce], ch="dbg")
                dbg_gate[0] = dbg_ev
            arr, evc = raw_stats(
                [oacc[:, lb * BD:lb * BD + 1024] for lb in range(bh)],
                [trio_free[0], trio_free[1]])
            rdv, rr, vv = arr(0), arr(4), arr(2)
            stdv, g0v, sclv, nmbv = arr(6), arr(7), arr(3), arr(8)
            Bv, v1024 = arr(1), arr(5)
            p.op("vector", lambda e, o=rdv: e.reciprocal(o, oacc[:, 1024:bh * BD:BD]),
                 waits=[evc], drain=False)
            p.op("vector", lambda e, o=rr, a=rdv: e.tensor_mul(o, a, a))
            ev1 = p.op("vector", lambda e, o=vv, a=v1024, b=rr: e.tensor_mul(o, a, b),
                       inc="ve")
            ev2 = p.op("scalar", lambda e, o=stdv, a=vv: e.activation(
                o, a, AF.Sqrt, bias=epsb[:, :], scale=c1024b[:, :]),
                waits=[ev1], inc="sc")
            p.op("vector", lambda e, o=g0v, a=stdv: e.reciprocal(o, a), waits=[ev2],
                 drain=False)
            p.op("vector", lambda e, o=sclv, a=rdv, b=g0v: e.tensor_mul(o, a, b))
            ev3 = p.op("vector", lambda e, o=nmbv, a=Bv, b=sclv: e.scalar_tensor_tensor(
                o, a, -0.25, b, ALU.mult, ALU.mult), inc="ve")
            for lb in range(bh):
                i = h * bh + lb
                lane = lb % 2
                scl = sclv[:, lb:lb + 1]
                nmb = nmbv[:, lb:lb + 1]
                evx = p.dma("sync", xqr[lane][:, :], xq[i * 128:(i + 1) * 128, :],
                            waits=[xqr_fin_last[lane], av_last_ev], ch=f"xqf{lane}")
                ev4 = p.op("gpsimd", lambda e, b=xqr[lane], n=nmb:
                           e.tensor_scalar(b[:, :], b[:, :], onew[:, :], n,
                                           ALU.mult, ALU.add),
                           waits=[evx, ev3], inc="gp")
                ev5 = p.op("vector", lambda e, o=xpb[lane], lb=lb, s=scl, b=xqr[lane]:
                           e.scalar_tensor_tensor(o[:, :], oacc[:, lb * BD:lb * BD + 1024],
                                                  s, b[:, :], ALU.mult, ALU.add),
                           waits=[ev4, xpb_last[lane]], inc="xf", drain=False)
                xqr_fin_last[lane] = ev5
                evw = p.dma("sync", xp_d[i * 128:(i + 1) * 128, :], xpb[lane][:, :],
                            waits=[ev5], ch=f"xpw{i}")
                xpw_ev[i] = evw
                xpb_last[lane] = evw
                fin_all_ev = ev5
            if dbg:
                p.dma("sync", dsl2[h], slots[:, :], waits=[fin_all_ev], ch="dbg")


        evw1 = [p.dma("gpsimd", w1s[k].ap().rearrange("p (dc e) -> p dc e", dc=4),
                      w1[k * 512:(k + 1) * 512, :].rearrange("(dc p) e -> p dc e", p=128),
                      waits=[xkb_last[k]], ch=f"wd1{k}") for k in range(2)]
        wgate = qfm_free_ev[0] if EC * tok >= 16384 else fin_all_ev
        evw2 = p.dma("gpsimd", w2s.rearrange("p (dc e) -> p dc e", dc=EC),
                     w2.rearrange("(dc p) e -> p dc e", p=128),
                     waits=[wgate], ch="wd2")
        evw3 = p.dma("gpsimd", w3s.rearrange("p (dc e) -> p dc e", dc=EC),
                     w3.rearrange("(dc p) e -> p dc e", p=128),
                     waits=[wgate], ch="wd3")
        p.op("tensor", lambda e: e.matmul(av_e0[0][:, 0:512], onesf[:, :],
                                          brow[:, 0:512], start=True, stop=True),
             waits=[ev_bias, ev_const, trio_free[0], trio_free[1]], inc="te")
        evb2 = p.op("tensor", lambda e: e.matmul(av_e1[0][:, 0:512], onesf[:, :],
                                                 brow[:, 512:1024], start=True, stop=True),
                    inc="te")
        p.op("vector", lambda e: e.tensor_copy(brep[:, 0:512], av_e0[0][:, :]),
             waits=[evb2, fin_all_ev], inc="ve")
        evbr = p.op("vector", lambda e: e.tensor_copy(brep[:, 512:1024], av_e1[0][:, :]),
                    inc="ve")

        if EC * tok >= 16384:
            NB = 8
            attn_mm_done = [ektm_last[0], ektm_last[1]]
            xpl_bufs = [pt[:, 0:2048].bitcast(F32), pt[:, 2048:4096].bitcast(F32),
                        ektm[1][:, 0:2048].bitcast(F32),
                        ektm[1][:, 2048:4096].bitcast(F32),
                        xpb[0], xpb[1], xqr[0], xqr[1]]
            xpl_last = [list(attn_mm_done) for _ in range(4)] +                        [[xpb_last[0]], [xpb_last[1]],
                        [xqr_fin_last[0]], [xqr_fin_last[1]]]
        else:
            NB = 4
            xpl_bufs = [xpb[0], xpb[1], xqr[0], xqr[1]]
            xpl_last = [[xpb_last[0]], [xpb_last[1]],
                        [xqr_fin_last[0]], [xqr_fin_last[1]]]
        ust_free = [None, None]
        sps_f = [sunit_exp.get(sunit - 2), sunit_exp.get(sunit - 1)]
        sps_f = [sps_f[(sunit % 2)], sps_f[1 - (sunit % 2)]]
        s_ps3 = [s_ps[0], s_ps[1], av_dn[0]]
        sps_g = [sps_f[0], sps_f[1], dn_free[0]]
        trio_last = [evbr, fin_all_ev]
        outb_free = [None, None]
        thb_free = [None, None]
        outw_cnt = 0
        for st in range(nst):
            lane = st % 2
            evxs = []
            for tl in range(4):
                t = 4 * st + tl
                xl = xpl_bufs[t % NB]
                evxs.append(p.dma("sync", xl[:, :], xp_d[t * 128:(t + 1) * 128, :],
                                  waits=[xpw_ev[t]] + xpl_last[t % NB],
                                  ch=f"xpl{t % NB}"))
            prm, rdy = stat_batch([xpl_bufs[(4 * st + tl) % NB] for tl in range(4)],
                                  evxs)
            ust_evs = []
            for tl in range(4):
                t = 4 * st + tl
                xl = xpl_bufs[t % NB]
                rstd, nmr = prm[tl]
                evn = p.op("gpsimd", lambda e, o=ust[lane], tl=tl, i=xl, r=rstd, n=nmr:
                           e.tensor_scalar(o[:, tl * D:(tl + 1) * D], i[:, :], r, n,
                                           ALU.mult, ALU.add),
                           waits=[rdy] + ([ust_free[lane]] if tl == 0 else []),
                           inc="gp")
                ust_evs.append(evn)
            ufm_done = None
            for tl in range(4):
                tev = None
                for ec in range(EC):
                    tev = p.op("tensor", lambda e, lane=lane, tl=tl, ec=ec:
                               e.transpose(tr_ps[:, ec * 128:(ec + 1) * 128],
                                           ust[lane][:, tl * D + ec * 128:tl * D + (ec + 1) * 128],
                                           idb[:, :]),
                               waits=([ust_evs[tl], trps_free[0]] if ec == 0 else []),
                               inc="te")
                ufm_done = p.op("vector", lambda e, lane=lane, tl=tl: e.tensor_copy(
                    ufm[lane].rearrange("p (dc q) -> p dc q", dc=EC)[:, :, tl * 128:(tl + 1) * 128],
                    tr_ps[:, :].rearrange("p (dc q) -> p dc q", dc=EC)),
                    waits=[tev], inc="ve", drain=False)
                trps_free[0] = ufm_done
            ust_free[lane] = tev
            h_done = None
            for stage in range(2):
                last_mm_stage = None
                for ec in range(EC):
                    kg = 16 * st + 8 * stage + ec
                    ps = s_ps3[kg % 3]
                    mm_ev = None
                    for dc in range(EC):
                        wts = []
                        if dc == 0:
                            wts = [sps_g[kg % 3]]
                            if stage == 0:
                                wts += [ufm_done, evw1[0], evw1[1]]
                            else:
                                wts += [evw2]
                                if ec == 0:
                                    wts.append(h_done)
                        if stage == 0:
                            def bmm(e, ps=ps, dc=dc, ec=ec, lane=lane):
                                wsl = w1s[dc // 4][:, (dc % 4) * 1024 + ec * 128:(dc % 4) * 1024 + (ec + 1) * 128]
                                return e.matmul(ps[:, :], wsl,
                                                ufm[lane][:, dc * 512:(dc + 1) * 512],
                                                start=(dc == 0), stop=(dc == EC - 1))
                        else:
                            def bmm(e, ps=ps, dc=dc, ec=ec):
                                wsl = w2s[:, dc * 1024 + ec * 128:dc * 1024 + (ec + 1) * 128]
                                return e.matmul(ps[:, :], wsl,
                                                h1f[:, dc * 512:(dc + 1) * 512],
                                                start=(dc == 0), stop=(dc == EC - 1))
                        mm_ev = p.op("tensor", bmm, waits=wts, inc="te")
                    last_mm_stage = mm_ev
                    dstf = h1f if stage == 0 else h2f
                    if kg % 2 == 1:
                        cp = p.op("vector", lambda e, ps=ps, dstf=dstf, ec=ec:
                                  e.tensor_copy(dstf[:, ec * 512:(ec + 1) * 512], ps[:, :]),
                                  waits=[mm_ev], inc="ve", drain=False)
                        sps_g[kg % 3] = cp
                        sps_g[(kg - 1) % 3] = cp
                        last_cp = cp
                    else:
                        p.op("vector", lambda e, ps=ps, dstf=dstf, ec=ec:
                             e.tensor_copy(dstf[:, ec * 512:(ec + 1) * 512], ps[:, :]),
                             waits=[mm_ev], drain=False)
                h_done = last_cp
            h2_done = h_done
            for tb in range(4):
                t = 4 * st + tb
                mm_ev = None
                for eh in range(2):
                    bank = (av_e0 if eh == 0 else av_e1)[tb % 2]
                    for dc in range(EC):
                        wts = []
                        if eh == 0 and dc == 0:
                            wts = [h2_done, evw3, trio_last[tb % 2]]
                        def bmm3(e, bank=bank, dc=dc, eh=eh, tb=tb):
                            return e.matmul(bank[:, 0:512],
                                            h2f[:, dc * 512 + tb * 128:dc * 512 + (tb + 1) * 128],
                                            w3s[:, dc * 1024 + eh * 512:dc * 1024 + (eh + 1) * 512],
                                            start=(dc == 0), stop=(dc == EC - 1))
                        mm_ev = p.op("tensor", bmm3, waits=wts, inc="te")
                p.op("vector", lambda e, o=thb[tb % 2], tb=tb:
                     e.tensor_add(o[:, 0:512], av_e0[tb % 2][:, :], brep[:, 0:512]),
                     waits=[mm_ev, thb_free[tb % 2]], inc="ve", drain=False)
                evba = p.op("vector", lambda e, o=thb[tb % 2], tb=tb:
                            e.tensor_add(o[:, 512:1024], av_e1[tb % 2][:, :], brep[:, 512:1024]),
                            inc="ve", drain=False)
                trio_last[tb % 2] = evba
                evth = p.op("scalar", lambda e, o=thb[tb % 2]:
                            e.activation(o[:, :], o[:, :], AF.Tanh),
                            waits=[evba], inc="sc")
                prm4, rdy4 = stat_batch([thb[tb % 2]], [evth])
                rstd4, nmr4 = prm4[0]
                xl = xpl_bufs[t % NB]
                ev6 = p.op("gpsimd", lambda e, b=xl, n=nmr4:
                           e.tensor_scalar(b[:, :], b[:, :], onew[:, :], n,
                                           ALU.mult, ALU.add),
                           waits=[rdy4], inc="gp")
                ev7 = p.op("vector", lambda e, o=outb[tb % 2], i=thb[tb % 2], r=rstd4, b=xl:
                           e.scalar_tensor_tensor(o[:, :], i[:, :], r, b[:, :],
                                                  ALU.mult, ALU.add),
                           waits=[ev6, outb_free[tb % 2]], inc="ve", drain=False)
                thb_free[tb % 2] = ev7
                xpl_last[t % NB] = [ev7]
                evo = p.dma("scalar", out[t * 128:(t + 1) * 128, :], outb[tb % 2][:, :],
                            waits=[ev7], ch="outw")
                outw_cnt += 1
                outb_free[tb % 2] = evo
        p.op("scalar", lambda e: e.nop(), waits=[("outw", 16 * outw_cnt)])

        p.emit(block)
    return nc


def _shard_rows(tok, j):
    nb = tok // 128
    return np.concatenate([np.arange(128 * (4 * i + j), 128 * (4 * i + j) + 128)
                           for i in range(nb)])


def _mask_for(j):
    c = np.arange(KC)[:, None, None]
    pp = np.arange(128)[None, :, None]
    f = np.arange(128)[None, None, :]
    m0 = (128 * c + pp <= 128 * j + f)
    m1 = (128 * (c - 4) + pp <= 128 * j + f)
    m = np.stack([m0, m1]).astype(ml_dtypes.bfloat16)
    return m.reshape(2 * KC * 128, 128)


def _run(x, R, W1, W2, W3, B, seq):
    tok = seq // 4
    nc = build_nc(seq, tok)
    in_maps = []
    rows_all = []
    for c in range(8):
        b, j = c // 4, c % 4
        rows = _shard_rows(tok, j)
        rows_all.append(rows)
        in_maps.append({
            "xq": np.ascontiguousarray(x[b][rows]),
            "xk": np.ascontiguousarray(x[b]),
            "rw": R, "w1": W1, "w2": W2, "w3": W3,
            "bi": np.ascontiguousarray(B.reshape(1, D)),
            "mk": _mask_for(j),
        })
    res = run_bass_kernel_spmd(nc, in_maps, core_ids=list(range(8)))
    outx = np.empty((2, seq, D), np.float32)
    for c in range(8):
        outx[c // 4][rows_all[c]] = res.results[c]["out"]
    return outx


def kernel(L0_attn_encoding, L0_attn_R, L0_ffn_weight, L0_ffn_aux_weight1,
           L0_ffn_aux_weight2, L0_ffn_bias):
    x = np.ascontiguousarray(np.asarray(L0_attn_encoding, dtype=np.float32))
    R = np.ascontiguousarray(np.asarray(L0_attn_R, dtype=np.float32))
    W1 = np.ascontiguousarray(np.asarray(L0_ffn_weight, dtype=np.float32))
    W2 = np.ascontiguousarray(np.asarray(L0_ffn_aux_weight1, dtype=np.float32))
    W3 = np.ascontiguousarray(np.asarray(L0_ffn_aux_weight2, dtype=np.float32))
    B = np.ascontiguousarray(np.asarray(L0_ffn_bias, dtype=np.float32))
    return _run(x, R, W1, W2, W3, B, seq=x.shape[1])
